# version 02 (final)
# baseline (speedup 1.0000x reference)
import math
import numpy as np
import concourse.bass as bass
import concourse.mybir as mybir
from concourse.bass_utils import run_bass_kernel_spmd

F32 = mybir.dt.float32
BF16 = mybir.dt.bfloat16
AF = mybir.ActivationFunctionType
ALU = mybir.AluOpType

DM, DI, DN, DR, DC, INTER = 2048, 4096, 16, 128, 4, 8192
NKD = DM // 128
NCK = DI // 128
NCI = INTER // 128
T = 512
TILES = 2
TPC = T * TILES
HW = T + 3
WFW = 2112
LNEPS = 1e-5
RS2048 = 1.0 / math.sqrt(2048.0)
RS8192 = 1.0 / math.sqrt(8192.0)

ENGINES = ("sync", "scalar", "vector", "tensor", "gpsimd")

_DEBUG = ()


def _hadamard(n):
    H = np.array([[1.0]], np.float32)
    while H.shape[0] < n:
        H = np.block([[H, H], [H, -H]])
    return H.astype(np.float32)


def _blockdiag(h, copies):
    n = h.shape[0]
    out = np.zeros((n * copies, n * copies), np.float32)
    for i in range(copies):
        out[i * n:(i + 1) * n, i * n:(i + 1) * n] = h
    return out


def _shuffle_perm(n):
    nk = n // 128
    perm = np.zeros(n, np.int64)
    for rp in range(n):
        q, k = divmod(rp, nk)
        perm[rp] = k * 128 + q
    return perm


class Ctx:
    def __init__(self, nc, en, E, sems):
        self.nc, self.en, self.E = nc, en, E
        self.sems = sems
        self.cnt = {k: 0 for k in sems}
        self.bar_n = 0
        self.bank_free = [None] * 8

    def on(self, en):
        return self.en == en

    def wait(self, en, sem, val=None):
        if self.en == en:
            self.E.wait_ge(self.sems[sem], self.cnt[sem] if val is None else val)

    def do(self, en, fn, sem=None, by=1):
        ins = fn(self.E) if self.en == en else None
        if sem is not None:
            self.cnt[sem] += by
            if ins is not None:
                ins.then_inc(self.sems[sem], by)
            return self.cnt[sem]
        return None

    def dma(self, en, out, in_, sem):
        return self.do(en, lambda E: E.dma_start(out=out, in_=in_), sem, 16)

    def barrier(self):
        self.bar_n += 1
        tgt = 5 * self.bar_n
        self.cnt["BAR"] += 5
        self.E.sem_inc(self.sems["BAR"], 1)
        self.E.wait_ge(self.sems["BAR"], tgt)

    def acquire_bank(self, b):
        ev = self.bank_free[b]
        if ev is not None:
            self.wait("tensor", ev[0], ev[1])

    def release_bank(self, b, sem):
        self.bank_free[b] = (sem, self.cnt[sem])


def _sb_ap(h, off, dims):
    return bass.AP(h, off, dims)


def _wxp_piece(t, c0, w):
    assert c0 % 160 == 0 and w % 160 == 0
    ck0 = c0 // 160
    ncks = w // 160
    return bass.AP(t, ck0 * 128 * 160, [[160, 128], [128 * 160, ncks], [1, 160]])


class Prog:
    def __init__(self, C: Ctx, v: dict):
        self.C, self.v = C, v
        self.wfree = [None, None]
        self.afree = [None, None]
        self.n_out_dma = 0
        self.da_last_scan = 0

    def mm(self, out_ap, lh, rh, start, stop, inc=False):
        C = self.C
        ins = None
        if C.on("tensor"):
            ins = C.E.matmul(out_ap, lh, rh, start=start, stop=stop)
        if inc:
            C.cnt["TEN"] += 1
            if ins is not None:
                ins.then_inc(C.sems["TEN"], 1)
            return C.cnt["TEN"]
        return None

    def mm3(self, out_ap, lh, ll, rh, rl, first, last, inc=False):
        self.mm(out_ap, lh, rh, start=first, stop=False)
        self.mm(out_ap, lh, rl, start=False, stop=False)
        return self.mm(out_ap, ll, rh, start=False, stop=last, inc=inc)

    def split(self, src_ap, hi_ap, lo_ap, scale=None):
        C = self.C
        if scale is None:
            c1 = C.do("scalar", lambda E: E.activation(out=hi_ap, in_=src_ap,
                                                       func=AF.Copy), "ACT")
            C.wait("vector", "ACT", c1)
            return C.do("vector", lambda E: E.tensor_sub(lo_ap, src_ap, hi_ap), "VEC")
        c1 = C.do("scalar", lambda E: E.activation(out=hi_ap, in_=src_ap,
                                                   func=AF.Copy, scale=scale), "ACT")
        C.wait("vector", "ACT", c1)
        return C.do("vector", lambda E: E.scalar_tensor_tensor(
            out=lo_ap, in0=src_ap, scalar=scale, in1=hi_ap,
            op0=ALU.mult, op1=ALU.subtract), "VEC")

    def wload(self, bi, dram_ap, n):
        C, v = self.C, self.v
        if self.wfree[bi] is not None:
            C.wait("sync", *self.wfree[bi])
        c = C.dma("sync", v["wf"][bi][:, 0:n], dram_ap, "SQ")
        C.wait("scalar", "SQ", c)
        C.wait("vector", "SQ", c)
        self.split(v["wf"][bi][:, 0:n], v["wh"][bi][:, 0:n], v["wl"][bi][:, 0:n])
        C.wait("tensor", "VEC")
        C.wait("tensor", "ACT")

    def wconsumed(self, bi):
        self.wfree[bi] = ("TEN", self.C.cnt["TEN"])

    def phase_consts(self):
        C, v = self.C, self.v
        off = 0
        for name, w in (("A", NCK * DN), ("dtb", NCK), ("Dsk", NCK),
                        ("cwT", NCK * DC), ("su_ug", NKD), ("su_dp", NCI),
                        ("wsu", NCI), ("wsg", NCI), ("wsd", NKD), ("gates", 16)):
            C.dma("gpsimd", v[name][:, :], v["aux"][:, off:off + w], "GQ")
            off += w
        for i, name in enumerate(("H128", "BD16", "BD64")):
            C.do("gpsimd", lambda E, i=i, name=name: E.dma_start(
                out=v[name][:, :], in_=v["hcst"][:, i * 128:(i + 1) * 128]), "GQ", 16)
        xpcols = [(0, 1920), (1920, 1920), (3840, 1280)]
        for c0, w in xpcols:
            C.dma("sync", v["wf"][0][:, 0:w],
                  _wxp_piece(v["w_xp"].tensor, c0, w), "SQ")
            C.wait("scalar", "SQ")
            C.wait("vector", "SQ")
            cs = self.split(v["wf"][0][:, 0:w], v["wxph"][:, c0:c0 + w],
                            v["wxpl"][:, c0:c0 + w])
            C.wait("sync", "VEC", cs)
        C.wait("vector", "GQ")
        C.do("vector", lambda E: E.memset(v["stt"][:, :], 0.0), "VEC")
        C.do("vector", lambda E: E.memset(v["hin"][:, :], 0.0), "VEC")
        C.do("vector", lambda E: E.memset(v["carry"][:, :], 0.0), "VEC")
        C.do("vector", lambda E: E.memset(v["totd"][:, :], 0.0), "VEC")
        C.do("vector", lambda E: E.memset(v["onesb"][:, :], 1.0), "VEC")
        for piece in range(2):
            c = C.dma("sync", v["wf"][0][:, :2048],
                      v["w_dt"][:, piece * 2048:(piece + 1) * 2048], "SQ")
            C.wait("scalar", "SQ", c)
            C.wait("vector", "SQ", c)
            cs = self.split(v["wf"][0][:, :2048],
                            v["wdth"][:, piece * 2048:(piece + 1) * 2048],
                            v["wdtl"][:, piece * 2048:(piece + 1) * 2048])
            C.wait("sync", "VEC", cs)
        C.barrier()

    def load_xpair(self, t):
        C, v = self.C, self.v
        W = HW if t == 0 else T
        colbase = 0 if t == 0 else HW
        xt = v["x_in"].tensor
        cl = 0
        for g in range(4):
            bi = 0
            if self.wfree[bi] is not None:
                C.wait("sync", *self.wfree[bi])
            src = bass.AP(xt, (4 * g * 128) * (TPC + 3) + colbase,
                          [[TPC + 3, 128], [128 * (TPC + 3), 4], [1, W]])
            dst = _sb_ap(v["wf"][bi], 0, [[WFW, 128], [W, 4], [1, W]])
            c = C.dma("sync", dst, src, "SQ")
            C.wait("scalar", "SQ", c)
            C.wait("vector", "SQ", c)
            for i in range(4):
                ck = 4 * g + i
                cl = self.split(v["wf"][bi][:, i * W:(i + 1) * W],
                                v["xph"][:, ck * HW:ck * HW + W],
                                v["xpl"][:, ck * HW:ck * HW + W])
            self.wfree[bi] = ("VEC", cl)
        self._xpair_ready = ("VEC", C.cnt["VEC"], "ACT", C.cnt["ACT"])

    def phase_front(self, t):
        C, v = self.C, self.v
        if t == 1 and v.get("__dbg_cr"):
            C.wait("gpsimd", "VEC")
            C.dma("gpsimd", v["scr_cr"], v["carry"][:, :], "GQ")
        self.load_xpair(t)
        xoff = 3 if t == 0 else 0
        GRP = 3
        spill_grp = []
        stage_par = 0
        self.afree = [None, None]
        C.acquire_bank(6)
        C.acquire_bank(7)
        first_xdbl = True
        for g0 in range(0, NCK, GRP):
            mis = list(range(g0, min(g0 + GRP, NCK)))
            nj = len(mis)
            for j in range(nj):
                C.acquire_bank(j)
                if t == 0:
                    C.acquire_bank(GRP + j)
            C.wait("tensor", self._xpair_ready[0], self._xpair_ready[1])
            C.wait("tensor", self._xpair_ready[2], self._xpair_ready[3])
            grp_done = None
            for k in range(NKD):
                bi = k % 2
                nm = nj * 128
                self.wload(bi, bass.AP(v["w_in_x"].tensor,
                                       k * 128 * DI + mis[0] * 128,
                                       [[DI, 128], [1, nm]]), nm)
                for j, mi in enumerate(mis):
                    lh = v["wh"][bi][:, j * 128:(j + 1) * 128]
                    ll = v["wl"][bi][:, j * 128:(j + 1) * 128]
                    rh = v["xph"][:, k * HW + xoff:k * HW + xoff + T]
                    rl = v["xpl"][:, k * HW + xoff:k * HW + xoff + T]
                    is_last_op = (j == nj - 1) and (t != 0)
                    r = self.mm3(v["psum"][j][:, :], lh, ll, rh, rl,
                                 first=(k == 0), last=(k == NKD - 1),
                                 inc=is_last_op)
                    if t == 0:
                        rhh = v["xph"][:, k * HW:k * HW + 3]
                        rlh = v["xpl"][:, k * HW:k * HW + 3]
                        r = self.mm3(v["psum"][GRP + j][:, 0:3], lh, ll, rhh, rlh,
                                     first=(k == 0), last=(k == NKD - 1),
                                     inc=(j == nj - 1))
                self.wconsumed(bi)
                if k == NKD - 1:
                    grp_done = C.cnt["TEN"]
            for j, mi in enumerate(mis):
                C.wait("vector", "TEN", grp_done)
                if t == 0:
                    C.do("vector", lambda E, j=j: E.tensor_copy(
                        v["xpc"][:, 0:3], v["psum"][GRP + j][:, 0:3]), "VEC")
                else:
                    C.do("vector", lambda E, mi=mi: E.tensor_copy(
                        v["xpc"][:, 0:3], v["carry"][:, mi * 3:mi * 3 + 3]), "VEC")
                C.do("vector", lambda E, j=j: E.tensor_copy(
                    v["xpc"][:, 3:3 + T], v["psum"][j][:, :]), "VEC")
                cxp = C.do("vector", lambda E, mi=mi, j=j: E.tensor_copy(
                    v["carry"][:, mi * 3:mi * 3 + 3], v["psum"][j][:, T - 3:T]), "VEC")
                C.release_bank(j, "VEC")
                if v.get("__dbg_xp"):
                    C.wait("gpsimd", "VEC", cxp)
                    C.dma("gpsimd",
                          bass.AP(v["scr_xp"].tensor, (t * 128) * (NCK * HW) + mi * HW,
                                  [[NCK * HW, 128], [1, HW]]),
                          v["xpc"][:, 0:HW], "GQ")
                if t == 0:
                    C.release_bank(GRP + j, "VEC")
                C.do("vector", lambda E, mi=mi: E.tensor_scalar_mul(
                    v["cvt"][:, :], v["xpc"][:, 0:T],
                    v["cwT"][:, mi * DC:mi * DC + 1]), "VEC")
                cc = None
                for w in range(1, DC):
                    cc = C.do("vector", lambda E, mi=mi, w=w: E.scalar_tensor_tensor(
                        out=v["cvt"][:, :], in0=v["xpc"][:, w:w + T],
                        scalar=v["cwT"][:, mi * DC + w:mi * DC + w + 1],
                        in1=v["cvt"][:, :], op0=ALU.mult, op1=ALU.add), "VEC")
                C.wait("scalar", "VEC", cc)
                cs = C.do("scalar", lambda E: E.activation(
                    out=v["evf"][:, :], in_=v["cvt"][:, :], func=AF.Silu), "ACT")
                s = len(spill_grp)
                par = stage_par
                if self.afree[par] is not None and s == 0:
                    C.wait("scalar", *self.afree[par])
                    C.wait("vector", *self.afree[par])
                C.wait("vector", "ACT", cs)
                csp = self.split(v["evf"][:, :],
                                 v["ash"][par][:, s * T:(s + 1) * T],
                                 v["asl"][par][:, s * T:(s + 1) * T])
                C.wait("tensor", "VEC", csp)
                C.wait("tensor", "ACT")
                lh1 = v["wxph"][:, mi * 160:mi * 160 + 128]
                ll1 = v["wxpl"][:, mi * 160:mi * 160 + 128]
                lh2 = v["wxph"][:, mi * 160 + 128:mi * 160 + 160]
                ll2 = v["wxpl"][:, mi * 160 + 128:mi * 160 + 160]
                rh = v["ash"][par][:, s * T:(s + 1) * T]
                rl = v["asl"][par][:, s * T:(s + 1) * T]
                self.mm3(v["psum"][6][:, :], lh1, ll1, rh, rl,
                         first=first_xdbl, last=(mi == NCK - 1))
                cx = self.mm3(v["psum"][7][0:32, :], lh2, ll2, rh, rl,
                              first=first_xdbl, last=(mi == NCK - 1), inc=True)
                first_xdbl = False
                spill_grp.append(mi)
                if len(spill_grp) == 4:
                    base = spill_grp[0] * T
                    xc = v["scr_xc"].tensor
                    C.wait("gpsimd", "TEN", cx)
                    for hl, buf in ((0, v["ash"][par]), (1, v["asl"][par])):
                        cg = C.dma("gpsimd",
                                   bass.AP(xc, ((t * 2 + hl) * 128) * (NCK * T) + base,
                                           [[NCK * T, 128], [1, 4 * T]]),
                                   buf[:, 0:4 * T], "GQ")
                    self.afree[par] = ("GQ", cg)
                    stage_par = 1 - par
                    spill_grp = []
        ct = C.cnt["TEN"]
        C.wait("scalar", "TEN", ct)
        C.wait("vector", "TEN", ct)
        self.split(v["psum"][6][:, :], v["dth"][t][:, :], v["dtl"][t][:, :])
        C.release_bank(6, "VEC")
        cb = C.do("vector", lambda E: E.tensor_copy(v["bcbuf"][:, :],
                                                    v["psum"][7][0:32, :]), "VEC")
        C.release_bank(7, "VEC")
        C.wait("gpsimd", "VEC", cb)
        bs = v["bscr"].tensor
        C.dma("gpsimd", bass.AP(bs, (t * 2 + 0) * DN * T, [[1, DN * T]]),
              v["bcbuf"][0:16, :], "GQ")
        C.dma("gpsimd", bass.AP(bs, (t * 2 + 1) * DN * T, [[1, DN * T]]),
              v["bcbuf"][16:32, :], "GQ")
        C.barrier()

    def phase_scan(self, t, s2):
        C, v = self.C, self.v
        self.afree = [None, None]
        self.da_last_scan = 0
        bs = v["bscr"].tensor
        if s2:
            C.wait("gpsimd", "ST", 16 * self.n_out_dma)
        cbr = C.dma("gpsimd", v["brep"][:, :],
                    bass.AP(bs, (t * 2 + 0) * DN * T, [[0, 128], [1, DN * T]]), "GQ")
        if s2:
            cbr = C.dma("gpsimd", v["crep"][:, :],
                        bass.AP(bs, (t * 2 + 1) * DN * T, [[0, 128], [1, DN * T]]),
                        "GQ")
        C.wait("vector", "GQ", cbr)
        state = v["hin"] if s2 else v["stt"]
        xc = v["scr_xc"].tensor
        for ckg in range(8):
            par = ckg % 2
            if self.afree[par] is not None:
                C.wait("gpsimd", *self.afree[par])
            base = ckg * 4 * T
            for hl, buf in ((0, v["ash"][par]), (1, v["asl"][par])):
                crl = C.dma("gpsimd", buf[:, 0:4 * T],
                            bass.AP(xc, ((t * 2 + hl) * 128) * (NCK * T) + base,
                                    [[NCK * T, 128], [1, 4 * T]]), "GQ")
            C.wait("vector", "GQ", crl)
            for s in range(4):
                ck = ckg * 4 + s
                b = ck % 6
                C.acquire_bank(b)
                self.mm3(v["psum"][b][:, :],
                         v["wdth"][:, ck * 128:(ck + 1) * 128],
                         v["wdtl"][:, ck * 128:(ck + 1) * 128],
                         v["dth"][t][:, :], v["dtl"][t][:, :],
                         first=True, last=True, inc=True)
                C.wait("scalar", "TEN")
                C.do("scalar", lambda E, b=b, ck=ck: E.activation(
                    out=v["e1b"][:, :], in_=v["psum"][b][:, :], func=AF.Exp,
                    bias=v["dtb"][:, ck:ck + 1]), "ACT")
                if not s2:
                    cd = C.do("scalar", lambda E, t=t, ck=ck: E.activation(
                        out=v["delta"][:, :], in_=v["e1b"][:, :], func=AF.Ln,
                        bias=1.0,
                        accum_out=v["totd"][:, t * NCK + ck:t * NCK + ck + 1]), "ACT")
                else:
                    cd = C.do("scalar", lambda E: E.activation(
                        out=v["delta"][:, :], in_=v["e1b"][:, :], func=AF.Ln,
                        bias=1.0), "ACT")
                C.release_bank(b, "ACT")
                C.do("vector", lambda E, par=par, s=s: E.tensor_add(
                    v["rec"][:, :], v["ash"][par][:, s * T:(s + 1) * T],
                    v["asl"][par][:, s * T:(s + 1) * T]), "VEC")
                C.wait("vector", "ACT", cd)
                C.do("vector", lambda E: E.tensor_mul(
                    v["du"][:, :], v["rec"][:, :], v["delta"][:, :]), "VEC")
                for ne in range(8):
                    for ni in range(2):
                        n = 2 * ne + ni
                        C.wait("scalar", "VEC", self.da_last_scan)
                        ca = C.do("scalar", lambda E, n=n, ck=ck: E.activation(
                            out=v["da"][:, :], in_=v["delta"][:, :],
                            func=AF.Exp,
                            scale=v["A"][:, ck * DN + n:ck * DN + n + 1]), "ACT")
                        C.do("vector", lambda E, n=n: E.tensor_mul(
                            v["db"][:, :], v["du"][:, :],
                            v["brep"][:, n * T:(n + 1) * T]), "VEC")
                        C.wait("vector", "ACT", ca)
                        if getattr(self, "_st_upd", None):
                            C.wait("vector", "GPC", self._st_upd)
                        self.da_last_scan = C.do(
                            "vector", lambda E, ni=ni, ck=ck, n=n, state=state:
                            E.tensor_tensor_scan(
                                out=v["hh"][:, ni * T:(ni + 1) * T],
                                data0=v["da"][:, :],
                                data1=v["db"][:, :],
                                initial=state[:, ck * DN + n:ck * DN + n + 1],
                                op0=ALU.mult, op1=ALU.add), "VEC")
                    C.wait("gpsimd", "VEC", self.da_last_scan)
                    cgp = C.do("gpsimd", lambda E, ck=ck, ne=ne, state=state:
                               E.tensor_copy(
                                   state[:, ck * DN + 2 * ne:ck * DN + 2 * ne + 2],
                                   _sb_ap(v["hh"], T - 1, [[2 * T, 128], [T, 2]])),
                               "GPC")
                    self._st_upd = cgp
                    if s2:
                        C.do("vector", lambda E, ne=ne: E.tensor_mul(
                            v["t2b"][:, :], v["hh"][:, :],
                            v["crep"][:, 2 * ne * T:(2 * ne + 2) * T]), "VEC")
                        if ne == 0:
                            C.do("vector", lambda E: E.tensor_add(
                                v["ytile"][:, :], v["t2b"][:, 0:T],
                                v["t2b"][:, T:2 * T]), "VEC")
                        else:
                            C.do("vector", lambda E: E.tensor_add(
                                v["ytile"][:, :], v["ytile"][:, :],
                                v["t2b"][:, 0:T]), "VEC")
                            C.do("vector", lambda E: E.tensor_add(
                                v["ytile"][:, :], v["ytile"][:, :],
                                v["t2b"][:, T:2 * T]), "VEC")
                if s2:
                    C.do("vector", lambda E, ck=ck: E.scalar_tensor_tensor(
                        out=v["ytile"][:, :], in0=v["rec"][:, :],
                        scalar=v["Dsk"][:, ck:ck + 1], in1=v["ytile"][:, :],
                        op0=ALU.mult, op1=ALU.add), "VEC")
                    if hasattr(self, "_yspill"):
                        C.wait("scalar", *self._yspill)
                        C.wait("vector", *self._yspill)
                    csp = self.split(v["ytile"][:, :], v["evh"][:, :], v["evl"][:, :])
                    C.wait("gpsimd", "VEC", csp)
                    C.wait("gpsimd", "ACT")
                    ys = v["scr_y"].tensor
                    for hl, b2 in ((0, v["evh"]), (1, v["evl"])):
                        cg = C.dma("gpsimd",
                                   bass.AP(ys,
                                           ((t * 2 + hl) * 128) * (NCK * T) + ck * T,
                                           [[NCK * T, 128], [1, T]]), b2[:, :], "GQ")
                    self._yspill = ("GQ", cg)
            self.afree[par] = ("VEC", C.cnt["VEC"])
        C.barrier()

    def phase_exchange(self):
        C, v = self.C, self.v
        C.do("vector", lambda E: E.tensor_add(
            v["totsum"][:, :], v["totd"][:, 0:NCK], v["totd"][:, NCK:2 * NCK]), "VEC")
        C.wait("scalar", "VEC")
        for ck in range(NCK):
            cp = C.do("scalar", lambda E, ck=ck: E.activation(
                out=v["t2b"][:, ck * DN:(ck + 1) * DN],
                in_=v["A"][:, ck * DN:(ck + 1) * DN], func=AF.Exp,
                scale=v["totsum"][:, ck:ck + 1]), "ACT")
        C.wait("gpsimd", "ACT", cp)
        C.wait("gpsimd", "VEC")
        cci, cco = v["cc_in"], v["cc_out"]
        C.dma("gpsimd", cci[:, 0:512], v["t2b"][:, 0:512], "GQ")
        cg = C.dma("gpsimd", cci[:, 512:1024], v["stt"][:, :], "GQ")
        C.wait("gpsimd", "GQ", cg)
        ccc = C.do("gpsimd", lambda E: E.collective_compute(
            "AllGather", ALU.bypass, replica_groups=[list(range(8))],
            ins=[cci.opt()], outs=[cco.opt()]), "CC")
        C.wait("gpsimd", "CC", ccc)
        cot = v["cc_out"].tensor
        cg = C.dma("gpsimd", v["r1"][:, 0:8192],
                   bass.AP(cot, 0, [[1024, 128], [128 * 1024, 8], [1, 1024]]), "GQ")
        C.wait("vector", "GQ", cg)
        for r in range(8):
            P = v["r1"][:, r * 1024:r * 1024 + 512]
            S = v["r1"][:, r * 1024 + 512:r * 1024 + 1024]
            C.do("vector", lambda E, P=P, r=r: E.tensor_scalar(
                out=v["da"][:, :], in0=P, scalar1=v["gates"][:, r:r + 1],
                scalar2=v["gates"][:, 8 + r:8 + r + 1], op0=ALU.mult, op1=ALU.add),
                "VEC")
            C.do("vector", lambda E: E.tensor_mul(
                v["db"][:, :], v["hin"][:, :], v["da"][:, :]), "VEC")
            C.do("vector", lambda E, S=S, r=r: E.tensor_scalar_mul(
                v["delta"][:, :], S, v["gates"][:, r:r + 1]), "VEC")
            C.do("vector", lambda E: E.tensor_add(
                v["hin"][:, :], v["db"][:, :], v["delta"][:, :]), "VEC")
        if v.get("__dbg_ex"):
            C.wait("gpsimd", "VEC")
            xt = v["scr_ex"].tensor
            C.dma("gpsimd", bass.AP(xt, 0, [[NCK * DN, 128], [1, NCK * DN]]),
                  v["t2b"][:, 0:512], "GQ")
            C.dma("gpsimd", bass.AP(xt, 128 * NCK * DN, [[NCK * DN, 128], [1, NCK * DN]]),
                  v["stt"][:, :], "GQ")
            C.dma("gpsimd", bass.AP(xt, 2 * 128 * NCK * DN, [[NCK * DN, 128], [1, NCK * DN]]),
                  v["hin"][:, :], "GQ")
        C.barrier()

    def phase_zgate(self, t):
        C, v = self.C, self.v
        self.load_xpair(t)
        self.afree = [None, None]
        xoff = 3 if t == 0 else 0
        ys = v["scr_y"].tensor
        yg = v["scr_yg"].tensor
        for ckg in range(8):
            par = ckg % 2
            if self.afree[par] is not None:
                C.wait("gpsimd", *self.afree[par])
            base = ckg * 4 * T
            for hl, buf in ((0, v["ash"][par]), (1, v["asl"][par])):
                crl = C.dma("gpsimd", buf[:, 0:4 * T],
                            bass.AP(ys, ((t * 2 + hl) * 128) * (NCK * T) + base,
                                    [[NCK * T, 128], [1, 4 * T]]), "GQ")
            C.wait("vector", "GQ", crl)
            for s in range(4):
                ck = ckg * 4 + s
                b = ck % 8
                C.acquire_bank(b)
                for k in range(NKD):
                    bi = k % 2
                    self.wload(bi, bass.AP(v["w_in_z"].tensor,
                                           k * 128 * DI + ck * 128,
                                           [[DI, 128], [1, 128]]), 128)
                    self.mm3(v["psum"][b][:, :], v["wh"][bi][:, 0:128],
                             v["wl"][bi][:, 0:128],
                             v["xph"][:, k * HW + xoff:k * HW + xoff + T],
                             v["xpl"][:, k * HW + xoff:k * HW + xoff + T],
                             first=(k == 0), last=(k == NKD - 1), inc=True)
                    self.wconsumed(bi)
                C.wait("scalar", "TEN")
                cz = C.do("scalar", lambda E, b=b: E.activation(
                    out=v["zs"][:, :], in_=v["psum"][b][:, :], func=AF.Silu), "ACT")
                C.release_bank(b, "ACT")
                C.do("vector", lambda E, par=par, s=s: E.tensor_add(
                    v["rec"][:, :], v["ash"][par][:, s * T:(s + 1) * T],
                    v["asl"][par][:, s * T:(s + 1) * T]), "VEC")
                C.wait("vector", "ACT", cz)
                C.do("vector", lambda E: E.tensor_mul(
                    v["evf"][:, :], v["rec"][:, :], v["zs"][:, :]), "VEC")
                if hasattr(self, "_ygspill"):
                    C.wait("scalar", *self._ygspill)
                    C.wait("vector", *self._ygspill)
                csp = self.split(v["evf"][:, :], v["evh"][:, :], v["evl"][:, :])
                C.wait("gpsimd", "VEC", csp)
                C.wait("gpsimd", "ACT")
                for hl, b2 in ((0, v["evh"]), (1, v["evl"])):
                    cg = C.dma("gpsimd",
                               bass.AP(yg, ((t * 2 + hl) * 128) * (NCK * T) + ck * T,
                                       [[NCK * T, 128], [1, T]]), b2[:, :], "GQ")
                self._ygspill = ("GQ", cg)
            self.afree[par] = ("VEC", C.cnt["VEC"])
        C.barrier()

    def phase_outproj(self, t):
        C, v = self.C, self.v
        self.afree = [None, None]
        yg = v["scr_yg"].tensor
        for mh in range(2):
            for j in range(8):
                C.acquire_bank(j)
            for ckg in range(8):
                par = ckg % 2
                if self.afree[par] is not None:
                    C.wait("gpsimd", *self.afree[par])
                base = ckg * 4 * T
                for hl, buf in ((0, v["ash"][par]), (1, v["asl"][par])):
                    crl = C.dma("gpsimd", buf[:, 0:4 * T],
                                bass.AP(yg, ((t * 2 + hl) * 128) * (NCK * T) + base,
                                        [[NCK * T, 128], [1, 4 * T]]), "GQ")
                C.wait("tensor", "GQ", crl)
                for s in range(4):
                    ck = ckg * 4 + s
                    bi = ck % 2
                    self.wload(bi, bass.AP(v["w_out"].tensor,
                                           ck * 128 * DM + mh * 1024,
                                           [[DM, 128], [1, 1024]]), 1024)
                    for j in range(8):
                        self.mm3(v["psum"][j][:, :],
                                 v["wh"][bi][:, j * 128:(j + 1) * 128],
                                 v["wl"][bi][:, j * 128:(j + 1) * 128],
                                 v["ash"][par][:, s * T:(s + 1) * T],
                                 v["asl"][par][:, s * T:(s + 1) * T],
                                 first=(ck == 0), last=(ck == NCK - 1),
                                 inc=(j == 7))
                    self.wconsumed(bi)
                self.afree[par] = ("TEN", C.cnt["TEN"])
            ct = C.cnt["TEN"]
            for j in range(8):
                mi = mh * 8 + j
                cg = C.dma("gpsimd", v["evf"][:, :],
                           bass.AP(v["x_in"].tensor,
                                   mi * 128 * (TPC + 3) + (3 if t == 0 else HW),
                                   [[TPC + 3, 128], [1, T]]), "GQ")
                C.wait("vector", "GQ", cg)
                C.wait("vector", "TEN", ct)
                cr = C.do("vector", lambda E, j=j, mi=mi: E.tensor_add(
                    v["r1"][:, mi * T:(mi + 1) * T], v["psum"][j][:, :],
                    v["evf"][:, :]), "VEC")
                C.release_bank(j, "VEC")
                C.wait("gpsimd", "VEC", cr)
        C.barrier()

    def phase_ln(self, t, which):
        C, v = self.C, self.v
        if which == 1 and v.get("__dbg_r2"):
            C.wait("gpsimd", "VEC")
            C.dma("gpsimd",
                  bass.AP(v["scr_r2"].tensor, (t * 128) * (NKD * T),
                          [[NKD * T, 128], [1, NKD * T]]), v["r1"][:, :], "GQ")
            C.wait("vector", "GQ")
        C.acquire_bank(6)
        C.acquire_bank(7)
        prev = None
        for mi in range(NKD):
            src = v["r1"][:, mi * T:(mi + 1) * T]
            if prev is not None:
                C.wait("scalar", "TEN", prev)
                C.wait("vector", "TEN", prev)
            c1 = self.split(src, v["r1h"][:, :], v["r1l"][:, :])
            C.wait("tensor", "VEC", c1)
            C.wait("tensor", "ACT")
            self.mm(v["psum"][6][0:1, :], v["onesb"][:, :], v["r1h"][:, :],
                    start=(mi == 0), stop=False)
            cmean = self.mm(v["psum"][6][0:1, :], v["onesb"][:, :], v["r1l"][:, :],
                            start=False, stop=(mi == NKD - 1), inc=True)
            C.wait("scalar", "TEN", cmean)
            C.wait("vector", "TEN", cmean)
            cq = C.do("scalar", lambda E, src=src: E.activation(
                out=v["sqf"][:, :], in_=src, func=AF.Square), "ACT")
            c2 = self.split(v["sqf"][:, :], v["r1h"][:, :], v["r1l"][:, :])
            C.wait("tensor", "VEC", c2)
            C.wait("tensor", "ACT")
            self.mm(v["psum"][7][0:1, :], v["onesb"][:, :], v["r1h"][:, :],
                    start=(mi == 0), stop=False)
            prev = self.mm(v["psum"][7][0:1, :], v["onesb"][:, :], v["r1l"][:, :],
                           start=False, stop=(mi == NKD - 1), inc=True)
        ct = C.cnt["TEN"]
        C.wait("vector", "TEN", ct)
        C.do("vector", lambda E: E.tensor_scalar_mul(
            v["ln1"][:, :], v["psum"][6][0:1, :], 1.0 / DM), "VEC")
        C.do("vector", lambda E: E.tensor_scalar_mul(
            v["ln2"][:, :], v["psum"][7][0:1, :], 1.0 / DM), "VEC")
        C.release_bank(6, "VEC")
        C.release_bank(7, "VEC")
        C.do("vector", lambda E: E.tensor_mul(
            v["lnt"][:, :], v["ln1"][:, :], v["ln1"][:, :]), "VEC")
        C.do("vector", lambda E: E.tensor_sub(
            v["ln2"][:, :], v["ln2"][:, :], v["lnt"][:, :]), "VEC")
        cv = C.do("vector", lambda E: E.tensor_scalar_add(
            v["ln2"][:, :], v["ln2"][:, :], LNEPS), "VEC")
        C.wait("scalar", "VEC", cv)
        C.do("scalar", lambda E: E.activation(
            out=v["lnt"][:, :], in_=v["ln2"][:, :], func=AF.Ln), "ACT")
        ce = C.do("scalar", lambda E: E.activation(
            out=v["ln2"][:, :], in_=v["lnt"][:, :], func=AF.Exp, scale=-0.5), "ACT")
        C.wait("vector", "ACT", ce)
        cm = C.do("vector", lambda E: E.tensor_mul(
            v["lnt"][:, :], v["ln1"][:, :], v["ln2"][:, :]), "VEC")
        C.wait("gpsimd", "VEC", cm)
        ls = v["lnscr"].tensor
        C.dma("gpsimd", bass.AP(ls, 0, [[1, T]]), v["ln2"][:, :], "GQ")
        C.dma("gpsimd", bass.AP(ls, T, [[1, T]]), v["lnt"][:, :], "GQ")
        cb = C.dma("gpsimd", v["lnb"][:, :],
                   bass.AP(ls, 0, [[0, 128], [1, 2 * T]]), "GQ")
        C.wait("vector", "GQ", cb)
        cn = None
        for mi in range(NKD):
            dst = v["r1"][:, mi * T:(mi + 1) * T]
            C.do("vector", lambda E, dst=dst: E.tensor_mul(
                dst, dst, v["lnb"][:, 0:T]), "VEC")
            cn = C.do("vector", lambda E, dst=dst: E.tensor_sub(
                dst, dst, v["lnb"][:, T:2 * T]), "VEC")
        if which == 1:
            C.wait("sync", "VEC", cn)
            self.n_out_dma += 1
            C.dma("sync",
                  bass.AP(v["out_t"].tensor, t * T,
                          [[TPC, 128], [128 * TPC, NKD], [1, T]]),
                  v["r1"][:, :], "ST")
        C.barrier()

    def phase_ffn_front(self, t):
        C, v = self.C, self.v
        if v.get("__dbg_x2"):
            C.wait("gpsimd", "VEC")
            C.dma("gpsimd",
                  bass.AP(v["scr_x2"].tensor, (t * 128) * (NKD * T),
                          [[NKD * T, 128], [1, NKD * T]]), v["r1"][:, :], "GQ")
        xq = v["scr_xq"].tensor
        for mi in range(NKD):
            C.do("vector", lambda E, mi=mi: E.tensor_scalar_mul(
                v["evf"][:, :], v["r1"][:, mi * T:(mi + 1) * T],
                v["su_ug"][:, mi:mi + 1]), "VEC")
            cs = self.split(v["evf"][:, :],
                            v["xph"][:, mi * HW:mi * HW + T],
                            v["xpl"][:, mi * HW:mi * HW + T])
            b = mi % 6
            C.acquire_bank(b)
            C.wait("tensor", "VEC", cs)
            C.wait("tensor", "ACT")
            self.mm(v["psum"][b][:, :], v["H128"][:, :],
                    v["xph"][:, mi * HW:mi * HW + T], start=True, stop=False)
            self.mm(v["psum"][b][:, :], v["H128"][:, :],
                    v["xpl"][:, mi * HW:mi * HW + T], start=False, stop=True,
                    inc=True)
            C.wait("scalar", "TEN")
            C.wait("vector", "TEN")
            if hasattr(self, "_xqspill"):
                C.wait("scalar", *self._xqspill)
                C.wait("vector", *self._xqspill)
            ce = self.split(v["psum"][b][:, :], v["evh"][:, :], v["evl"][:, :])
            C.release_bank(b, "VEC")
            C.wait("gpsimd", "VEC", ce)
            C.wait("gpsimd", "ACT")
            for hl, b2 in ((0, v["evh"]), (1, v["evl"])):
                cg = C.dma("gpsimd",
                           bass.AP(xq, ((t * 2 + hl) * 128) * (NKD * T) + mi * T,
                                   [[NKD * T, 128], [1, T]]), b2[:, :], "GQ")
            self._xqspill = ("GQ", cg)
        for j in range(NKD):
            if hasattr(self, "_bd16mm"):
                C.wait("gpsimd", "TEN", self._bd16mm)
            for hl, buf in ((0, v["evh"]), (1, v["evl"])):
                crl = C.dma("gpsimd", buf[:, :],
                            bass.AP(xq, ((t * 2 + hl) * 128) * (NKD * T)
                                    + 8 * j * (NKD * T),
                                    [[NKD * T, 8], [T, 16], [1, T]]), "GQ")
            b = j % 6
            C.acquire_bank(b)
            C.wait("tensor", "GQ", crl)
            self.mm(v["psum"][b][:, :], v["BD16"][:, :], v["evh"][:, :],
                    start=True, stop=False)
            self._bd16mm = self.mm(v["psum"][b][:, :], v["BD16"][:, :],
                                   v["evl"][:, :], start=False, stop=True, inc=True)
            C.wait("scalar", "TEN")
            C.wait("vector", "TEN")
            cs = self.split(v["psum"][b][:, :],
                            v["xph"][:, j * HW:j * HW + T],
                            v["xpl"][:, j * HW:j * HW + T], scale=RS2048)
            C.release_bank(b, "VEC")
            if v.get("__dbg_xq2"):
                C.wait("gpsimd", "VEC", cs)
                C.wait("gpsimd", "ACT")
                for hl, b3 in ((0, v["xph"]), (1, v["xpl"])):
                    C.dma("gpsimd",
                          bass.AP(v["scr_xq2"].tensor,
                                  ((t * 2 + hl) * 128) * (NKD * T) + j * T,
                                  [[NKD * T, 128], [1, T]]),
                          b3[:, j * HW:j * HW + T], "GQ")
        C.barrier()

    def phase_upgate(self, t):
        C, v = self.C, self.v
        hs = v["scr_h"].tensor
        for mg in range(8):
            for half, wnm, scl, bufn in ((0, "w_up", "wsu", "brep"),
                                         (1, "w_gate", "wsg", "crep")):
                for j in range(8):
                    C.acquire_bank(j)
                for k in range(NKD):
                    bi = k % 2
                    self.wload(bi, bass.AP(v[wnm].tensor,
                                           k * 128 * INTER + mg * 1024,
                                           [[INTER, 128], [1, 1024]]), 1024)
                    for j in range(8):
                        self.mm3(v["psum"][j][:, :],
                                 v["wh"][bi][:, j * 128:(j + 1) * 128],
                                 v["wl"][bi][:, j * 128:(j + 1) * 128],
                                 v["xph"][:, k * HW:k * HW + T],
                                 v["xpl"][:, k * HW:k * HW + T],
                                 first=(k == 0), last=(k == NKD - 1),
                                 inc=(j == 7))
                    self.wconsumed(bi)
                ct = C.cnt["TEN"]
                C.wait("scalar", "TEN", ct)
                for j in range(8):
                    ci = mg * 8 + j
                    if half == 0:
                        C.do("scalar", lambda E, j=j, ci=ci: E.activation(
                            out=v["ubuf"][:, j * T:(j + 1) * T],
                            in_=v["psum"][j][:, :], func=AF.Copy,
                            scale=v["wsu"][:, ci:ci + 1]), "ACT")
                    else:
                        C.do("scalar", lambda E, j=j, ci=ci: E.activation(
                            out=v["gbuf"][:, j * T:(j + 1) * T],
                            in_=v["psum"][j][:, :], func=AF.Silu,
                            scale=v["wsg"][:, ci:ci + 1]), "ACT")
                    C.release_bank(j, "ACT")
            for j in range(8):
                ci = mg * 8 + j
                C.wait("vector", "ACT")
                ch = C.do("vector", lambda E, j=j: E.tensor_mul(
                    v["evf"][:, :], v["gbuf"][:, j * T:(j + 1) * T],
                    v["ubuf"][:, j * T:(j + 1) * T]), "VEC")
                if hasattr(self, "_hspill"):
                    C.wait("scalar", *self._hspill)
                    C.wait("vector", *self._hspill)
                cs = self.split(v["evf"][:, :], v["evh"][:, :], v["evl"][:, :],
                                scale=v["su_dp"][:, ci:ci + 1])
                C.wait("gpsimd", "VEC", cs)
                C.wait("gpsimd", "ACT")
                for hl, b2 in ((0, v["evh"]), (1, v["evl"])):
                    cg = C.dma("gpsimd",
                               bass.AP(hs, ((t * 2 + hl) * 128) * (NCI * T) + ci * T,
                                       [[NCI * T, 128], [1, T]]), b2[:, :], "GQ")
                self._hspill = ("GQ", cg)
        C.barrier()

    def phase_fwht8k(self, t):
        C, v = self.C, self.v
        hs = v["scr_h"].tensor
        gs = v["scr_g"].tensor
        fs = v["scr_f"].tensor
        self.afree = [None, None]
        for cig in range(NCI // 4):
            par = cig % 2
            pg = 1 - par
            if self.afree[par] is not None:
                C.wait("gpsimd", *self.afree[par])
            base = cig * 4 * T
            for hl, buf in ((0, v["ash"][par]), (1, v["asl"][par])):
                crl = C.dma("gpsimd", buf[:, 0:4 * T],
                            bass.AP(hs, ((t * 2 + hl) * 128) * (NCI * T) + base,
                                    [[NCI * T, 128], [1, 4 * T]]), "GQ")
            C.wait("tensor", "GQ", crl)
            for s in range(4):
                b = s
                C.acquire_bank(b)
                self.mm(v["psum"][b][:, :], v["H128"][:, :],
                        v["ash"][par][:, s * T:(s + 1) * T], start=True, stop=False)
                self.mm(v["psum"][b][:, :], v["H128"][:, :],
                        v["asl"][par][:, s * T:(s + 1) * T], start=False, stop=True,
                        inc=(s == 3))
            ct = C.cnt["TEN"]
            C.wait("scalar", "TEN", ct)
            C.wait("vector", "TEN", ct)
            if self.afree[pg] is not None:
                C.wait("scalar", *self.afree[pg])
                C.wait("vector", *self.afree[pg])
            cs = None
            for s in range(4):
                cs = self.split(v["psum"][s][:, :],
                                v["ash"][pg][:, s * T:(s + 1) * T],
                                v["asl"][pg][:, s * T:(s + 1) * T])
                C.release_bank(s, "VEC")
            C.wait("gpsimd", "VEC", cs)
            C.wait("gpsimd", "ACT")
            for hl, buf in ((0, v["ash"][pg]), (1, v["asl"][pg])):
                cg = C.dma("gpsimd",
                           bass.AP(gs, ((t * 2 + hl) * 128) * (NCI * T) + base,
                                   [[NCI * T, 128], [1, 4 * T]]), buf[:, 0:4 * T],
                           "GQ")
            self.afree[par] = ("GQ", cg)
            self.afree[pg] = ("GQ", cg)
        spill = []
        stage_par = 0
        self.afree = [None, None]
        self._fwshf = None
        for j in range(NCI):
            if self._fwshf is not None:
                C.wait("gpsimd", *self._fwshf)
            for hl, buf in ((0, v["evh"]), (1, v["evl"])):
                crl = C.dma("gpsimd", buf[:, :],
                            bass.AP(gs, ((t * 2 + hl) * 128) * (NCI * T)
                                    + 2 * j * (NCI * T),
                                    [[NCI * T, 2], [T, 64], [1, T]]), "GQ")
            b = 4 + (j % 4)
            C.acquire_bank(b)
            C.wait("tensor", "GQ", crl)
            self.mm(v["psum"][b][:, :], v["BD64"][:, :], v["evh"][:, :],
                    start=True, stop=False)
            cmt = self.mm(v["psum"][b][:, :], v["BD64"][:, :], v["evl"][:, :],
                          start=False, stop=True, inc=True)
            self._fwshf = ("TEN", cmt)
            C.wait("scalar", "TEN", cmt)
            C.wait("vector", "TEN", cmt)
            s = len(spill)
            par = stage_par
            if self.afree[par] is not None and s == 0:
                C.wait("scalar", *self.afree[par])
                C.wait("vector", *self.afree[par])
            cs = self.split(v["psum"][b][:, :],
                            v["ash"][par][:, s * T:(s + 1) * T],
                            v["asl"][par][:, s * T:(s + 1) * T], scale=RS8192)
            C.release_bank(b, "VEC")
            spill.append(j)
            if len(spill) == 4:
                base = spill[0] * T
                C.wait("gpsimd", "VEC", cs)
                C.wait("gpsimd", "ACT")
                for hl, buf in ((0, v["ash"][par]), (1, v["asl"][par])):
                    cg = C.dma("gpsimd",
                               bass.AP(fs, ((t * 2 + hl) * 128) * (NCI * T) + base,
                                       [[NCI * T, 128], [1, 4 * T]]), buf[:, 0:4 * T],
                               "GQ")
                self.afree[par] = ("GQ", cg)
                stage_par = 1 - par
                spill = []
        C.barrier()

    def phase_down(self, t):
        C, v = self.C, self.v
        fs = v["scr_f"].tensor
        self.afree = [None, None]
        for mh in range(2):
            for j in range(8):
                C.acquire_bank(j)
            for kg in range(NCI // 4):
                par = kg % 2
                if self.afree[par] is not None:
                    C.wait("gpsimd", *self.afree[par])
                base = kg * 4 * T
                for hl, buf in ((0, v["ash"][par]), (1, v["asl"][par])):
                    crl = C.dma("gpsimd", buf[:, 0:4 * T],
                                bass.AP(fs, ((t * 2 + hl) * 128) * (NCI * T) + base,
                                        [[NCI * T, 128], [1, 4 * T]]), "GQ")
                C.wait("tensor", "GQ", crl)
                for s in range(4):
                    kk = kg * 4 + s
                    bi = kk % 2
                    self.wload(bi, bass.AP(v["w_dn"].tensor,
                                           kk * 128 * DM + mh * 1024,
                                           [[DM, 128], [1, 1024]]), 1024)
                    for j in range(8):
                        self.mm3(v["psum"][j][:, :],
                                 v["wh"][bi][:, j * 128:(j + 1) * 128],
                                 v["wl"][bi][:, j * 128:(j + 1) * 128],
                                 v["ash"][par][:, s * T:(s + 1) * T],
                                 v["asl"][par][:, s * T:(s + 1) * T],
                                 first=(kk == 0), last=(kk == NCI - 1),
                                 inc=(j == 7))
                    self.wconsumed(bi)
                self.afree[par] = ("TEN", C.cnt["TEN"])
            ct = C.cnt["TEN"]
            C.wait("scalar", "TEN", ct)
            for j in range(8):
                mi = mh * 8 + j
                ce = C.do("scalar", lambda E, j=j, mi=mi: E.activation(
                    out=v["evf"][:, :], in_=v["psum"][j][:, :], func=AF.Copy,
                    scale=v["wsd"][:, mi:mi + 1]), "ACT")
                C.wait("vector", "ACT", ce)
                cr = C.do("vector", lambda E, mi=mi: E.tensor_add(
                    v["r1"][:, mi * T:(mi + 1) * T], v["r1"][:, mi * T:(mi + 1) * T],
                    v["evf"][:, :]), "VEC")
                C.release_bank(j, "VEC")
                C.wait("scalar", "VEC", cr)
        C.barrier()

    def phase_end(self):
        C = self.C
        C.wait("sync", "ST", 16 * self.n_out_dma)


def _emit_program(C: Ctx, v):
    p = Prog(C, v)
    p.phase_consts()
    for t in range(TILES):
        p.phase_front(t)
        p.phase_scan(t, s2=False)
    p.phase_exchange()
    for t in range(TILES):
        p.phase_scan(t, s2=True)
        p.phase_zgate(t)
        p.phase_outproj(t)
        p.phase_ln(t, which=0)
        p.phase_ffn_front(t)
        p.phase_upgate(t)
        p.phase_fwht8k(t)
        p.phase_down(t)
        p.phase_ln(t, which=1)
    p.phase_end()


def build_nc(debug=()):
    nc = bass.Bass("TRN2", target_bir_lowering=False)
    v = {}

    def di(name, ap):
        v[name] = ap
        return ap

    di("x_in", nc.dram_tensor("x_in", [DM, TPC + 3], F32, kind="ExternalInput").ap())
    di("w_in_x", nc.dram_tensor("w_in_x", [DM, DI], F32, kind="ExternalInput").ap())
    di("w_in_z", nc.dram_tensor("w_in_z", [DM, DI], F32, kind="ExternalInput").ap())
    di("w_xp", nc.dram_tensor("w_xp", [DI, 160], F32, kind="ExternalInput").ap())
    di("w_dt", nc.dram_tensor("w_dt", [DR, DI], F32, kind="ExternalInput").ap())
    di("w_out", nc.dram_tensor("w_out", [DI, DM], F32, kind="ExternalInput").ap())
    di("w_up", nc.dram_tensor("w_up", [DM, INTER], F32, kind="ExternalInput").ap())
    di("w_gate", nc.dram_tensor("w_gate", [DM, INTER], F32, kind="ExternalInput").ap())
    di("w_dn", nc.dram_tensor("w_dn", [INTER, DM], F32, kind="ExternalInput").ap())
    di("aux", nc.dram_tensor("aux", [128, 2048], F32, kind="ExternalInput").ap())
    di("hcst", nc.dram_tensor("hcst", [128, 3 * 128], F32, kind="ExternalInput").ap())
    di("out_t", nc.dram_tensor("out", [DM, TPC], F32, kind="ExternalOutput").ap())

    def scratch(name, shape, dtype=BF16):
        kind = "ExternalOutput" if name in debug else "Internal"
        return di(name, nc.dram_tensor(name, shape, dtype, kind=kind).ap())

    scratch("scr_xc", [TILES, 2, 128, NCK * T])
    if "scr_xp" in debug:
        scratch("scr_xp", [TILES, 128, NCK * HW], F32)
    v["__dbg_xp"] = "scr_xp" in debug
    if "scr_cr" in debug:
        scratch("scr_cr", [128, NCK * 3], F32)
    v["__dbg_cr"] = "scr_cr" in debug
    if "scr_x2" in debug:
        scratch("scr_x2", [TILES, 128, NKD * T], F32)
    v["__dbg_x2"] = "scr_x2" in debug
    if "scr_xq2" in debug:
        scratch("scr_xq2", [TILES, 2, 128, NKD * T])
    v["__dbg_xq2"] = "scr_xq2" in debug
    if "scr_r2" in debug:
        scratch("scr_r2", [TILES, 128, NKD * T], F32)
    v["__dbg_r2"] = "scr_r2" in debug
    if "scr_ex" in debug:
        scratch("scr_ex", [3, 128, NCK * DN], F32)
    v["__dbg_ex"] = "scr_ex" in debug
    scratch("scr_y", [TILES, 2, 128, NCK * T])
    scratch("scr_yg", [TILES, 2, 128, NCK * T])
    scratch("scr_xq", [TILES, 2, 128, NKD * T])
    scratch("scr_h", [TILES, 2, 128, NCI * T])
    scratch("scr_g", [TILES, 2, 128, NCI * T])
    scratch("scr_f", [TILES, 2, 128, NCI * T])
    scratch("bscr", [TILES, 2, DN * T], F32)
    scratch("lnscr", [2, T], F32)
    scratch("cc_in", [128, 1024], F32)
    scratch("cc_out", [8 * 128, 1024], F32)

    def sb(name, shape, dtype=F32):
        h = nc.alloc_sbuf_tensor(name, shape, dtype)
        v[name] = h
        return h

    sb("A", [128, NCK * DN])
    sb("dtb", [128, NCK])
    sb("Dsk", [128, NCK])
    sb("cwT", [128, NCK * DC])
    sb("su_ug", [128, NKD])
    sb("su_dp", [128, NCI])
    sb("wsu", [128, NCI])
    sb("wsg", [128, NCI])
    sb("wsd", [128, NKD])
    sb("gates", [128, 16])
    sb("H128", [128, 128], BF16)
    sb("BD16", [128, 128], BF16)
    sb("BD64", [128, 128], BF16)
    sb("onesb", [128, 1], BF16)
    sb("wxph", [128, NCK * 160], BF16)
    sb("wxpl", [128, NCK * 160], BF16)
    sb("wdth", [128, DI], BF16)
    sb("wdtl", [128, DI], BF16)
    v["dth"] = [sb("dth0", [128, T], BF16), sb("dth1", [128, T], BF16)]
    v["dtl"] = [sb("dtl0", [128, T], BF16), sb("dtl1", [128, T], BF16)]
    sb("totd", [128, 2 * NCK])
    sb("totsum", [128, NCK])
    sb("stt", [128, NCK * DN])
    sb("hin", [128, NCK * DN])
    sb("xph", [128, NKD * HW], BF16)
    sb("xpl", [128, NKD * HW], BF16)
    sb("carry", [128, NCK * 3])
    v["wf"] = [sb("wf0", [128, WFW]), sb("wf1", [128, 1024])]
    v["wh"] = [sb("wh0", [128, 1024], BF16), sb("wh1", [128, 1024], BF16)]
    v["wl"] = [sb("wl0", [128, 1024], BF16), sb("wl1", [128, 1024], BF16)]
    v["ash"] = [sb("ash0", [128, 4 * T], BF16), sb("ash1", [128, 4 * T], BF16)]
    v["asl"] = [sb("asl0", [128, 4 * T], BF16), sb("asl1", [128, 4 * T], BF16)]
    arena = sb("arena", [128, 2 * DN * T])
    v["r1"] = arena[:, 0:NKD * T]
    v["brep"] = arena[:, 0:DN * T]
    v["crep"] = arena[:, DN * T:2 * DN * T]
    v["ubuf"] = arena[:, DN * T:DN * T + 8 * T]
    v["gbuf"] = arena[:, DN * T + 8 * T:2 * DN * T]
    sb("da", [128, T])
    sb("db", [128, T])
    sb("hh", [128, 2 * T])
    sb("delta", [128, T])
    sb("e1b", [128, T])
    sb("rec", [128, T])
    sb("du", [128, T])
    sb("t2b", [128, 2 * T])
    sb("evh", [128, T], BF16)
    sb("evl", [128, T], BF16)
    sb("evf", [128, T])
    sb("cvt", [128, T])
    v["zs"] = v["e1b"]
    v["ytile"] = v["cvt"]
    v["sqf"] = v["du"]
    v["lnb"] = v["hh"]
    v["r1h"] = v["evh"]
    v["r1l"] = v["evl"]
    v["xpc"] = v["t2b"][:, 0:HW + 1]
    v["bcbuf"] = v["hh"][0:32, 0:T]
    v["ln1"] = v["delta"][0:1, 0:T]
    v["ln2"] = v["e1b"][0:1, 0:T]
    v["lnt"] = v["rec"][0:1, 0:T]
    v["psum"] = [nc.alloc_psum_tensor(f"pb{i}", [128, 512], F32) for i in range(8)]

    sem_names = ["SQ", "GQ", "TEN", "VEC", "ACT", "GPC", "CC", "BAR", "ST"]
    sems = {}
    cms = []
    for s in sem_names:
        cm = nc.semaphore(s)
        sems[s] = cm.__enter__()
        cms.append(cm)

    with nc.Block() as block:
        for en in ENGINES:
            getattr(block, en)(lambda E, en=en: _emit_program(Ctx(nc, en, E, sems), v))

    for cm in cms:
        cm.__exit__(None, None, None)
    return nc



def _prep_inputs(inputs):
    f32 = np.float32
    hs = np.ascontiguousarray(np.asarray(inputs["L0_attn_hidden_states"], f32)
                              .reshape(8192, DM))
    Win = np.asarray(inputs["L0_attn_in_proj_weight"], f32)
    w_in_x = np.ascontiguousarray(Win[:DI].T)
    w_in_z = np.ascontiguousarray(Win[DI:].T)
    w_xp = np.ascontiguousarray(np.asarray(inputs["L0_attn_x_proj_weight"], f32).T)
    w_dt = np.ascontiguousarray(np.asarray(inputs["L0_attn_dt_proj_weight"], f32).T)
    w_out = np.ascontiguousarray(np.asarray(inputs["L0_attn_out_proj_weight"], f32).T)
    p2 = _shuffle_perm(2048)
    p8 = _shuffle_perm(8192)
    w_up = np.ascontiguousarray(np.asarray(inputs["L0_ffn_W_up"], f32).T[p2])
    w_gate = np.ascontiguousarray(np.asarray(inputs["L0_ffn_W_gate"], f32).T[p2])
    w_dn = np.ascontiguousarray(np.asarray(inputs["L0_ffn_W_down"], f32).T[p8])

    def colmajor(vec):
        return np.ascontiguousarray(np.asarray(vec, f32).reshape(-1, 128).T)

    aux = np.zeros((128, 2048), f32)
    off = 0

    def put(arr):
        nonlocal off
        w = arr.shape[1]
        aux[:, off:off + w] = arr
        off += w

    Alog = np.asarray(inputs["L0_attn_A_log"], f32)
    A = (-np.exp(Alog)).reshape(NCK, 128, DN).transpose(1, 0, 2).reshape(128, NCK * DN)
    put(np.ascontiguousarray(A))
    put(colmajor(inputs["L0_attn_dt_proj_bias"]))
    put(colmajor(inputs["L0_attn_D"]))
    cw = np.asarray(inputs["L0_attn_conv1d_weight"], f32)
    cwT = cw.reshape(NCK, 128, DC).transpose(1, 0, 2).reshape(128, NCK * DC)
    put(np.ascontiguousarray(cwT))
    put(colmajor(inputs["L0_ffn_SU_ug"]))
    put(colmajor(inputs["L0_ffn_SU_dp"]))
    ws = np.asarray(inputs["L0_ffn_Wscale_ug"], f32)
    put(colmajor(ws[:INTER]))
    put(colmajor(ws[INTER:]))
    put(colmajor(inputs["L0_ffn_Wscale_dp"]))
    gates_slot = off

    hcst = np.zeros((128, 3 * 128), f32)
    hcst[:, 0:128] = _hadamard(128)
    hcst[:, 128:256] = _blockdiag(_hadamard(16), 8)
    hcst[:, 256:384] = _blockdiag(_hadamard(64), 2)

    in_maps = []
    for c in range(8):
        t0 = c * TPC
        x = np.zeros((DM, TPC + 3), f32)
        lo = t0 - 3
        if c % 4 == 0:
            x[:, 3:] = hs[t0:t0 + TPC].T
        else:
            x[:, :] = hs[lo:t0 + TPC].T
        auxc = aux.copy()
        g = np.zeros(8, f32)
        for r in range(8):
            if r // 4 == c // 4 and r < c:
                g[r] = 1.0
        auxc[:, gates_slot:gates_slot + 8] = np.broadcast_to(g, (128, 8))
        auxc[:, gates_slot + 8:gates_slot + 16] = np.broadcast_to(1.0 - g, (128, 8))
        in_maps.append({
            "x_in": np.ascontiguousarray(x), "w_in_x": w_in_x, "w_in_z": w_in_z,
            "w_xp": w_xp, "w_dt": w_dt, "w_out": w_out, "w_up": w_up,
            "w_gate": w_gate, "w_dn": w_dn, "aux": auxc, "hcst": hcst,
        })
    return in_maps


def kernel(**inputs):
    in_maps = _prep_inputs(inputs)
    nc = build_nc(debug=_DEBUG)
    res = run_bass_kernel_spmd(nc, in_maps, core_ids=list(range(8)))
    out = np.zeros((8192, DM), np.float32)
    for c in range(8):
        out[c * TPC:(c + 1) * TPC] = np.asarray(res.results[c]["out"]).T
    return out.reshape(2, 4096, DM)
